# version 01
# baseline (speedup 1.0000x reference)
import numpy as np
import ml_dtypes
from contextlib import ExitStack

import concourse.bass as bass
import concourse.mybir as mybir
from concourse.bass_utils import run_bass_kernel_spmd

F32 = mybir.dt.float32
BF16 = mybir.dt.bfloat16
F16 = mybir.dt.float16
AF = mybir.ActivationFunctionType
OP = mybir.AluOpType
AX = mybir.AxisListType.X
BF = ml_dtypes.bfloat16

NCORES = 8
DETECT_RACE = True
D = 2048
KC = D // 128
TPC = 2048
TT = 512
NTILES = TPC // TT
HID = 8192
NHEAD, HDIM, CHUNK = 16, 128, 256
ENGINES = ("sync", "vector", "scalar", "tensor", "gpsimd")


class _EngRec:

    def __init__(self, eng, ctx):
        self._eng = eng
        self._ctx = ctx

    def __getattr__(self, name):
        fn = getattr(self._eng, name)
        if not callable(fn):
            return fn
        ctx = self._ctx

        def wrap(*a, **kw):
            r = fn(*a, **kw)
            ctx.last = r
            return r
        return wrap


class _Ctx:
    def __init__(self, eng, dsem):
        self.raw = eng
        self.e = _EngRec(eng, self)
        self.dsem = dsem
        self.ndma = 0
        self.last = None

    def dma(self, out, in_, slow=False):
        self.raw.dma_start(out=out, in_=in_,
                           allow_slow_non_contiguous=slow).then_inc(
            self.dsem, 16)
        self.ndma += 1


class SP:

    def __init__(self, nc):
        self.nc = nc
        self.stages = []

    def stage(self, name=""):
        st = {e: [] for e in ENGINES}
        self.stages.append(st)

        class Rec:
            pass

        r = Rec()
        for e in ENGINES:
            def add(fn, e=e, st=st):
                st[e].append(fn)
                return fn
            setattr(r, e, add)
        return r

    def emit(self, ctx):
        nc = self.nc
        stg = ctx.enter_context(nc.semaphore("stg"))
        dsems = {e: ctx.enter_context(nc.semaphore(f"dsem_{e}"))
                 for e in ENGINES}
        parts = [[e for e in ENGINES if st[e]] for st in self.stages]
        thr = []
        tot = 0
        for p in parts:
            tot += len(p)
            thr.append(tot)
        block = ctx.enter_context(nc.Block())
        for ename in ENGINES:
            def body(eng, ename=ename):
                c = _Ctx(eng, dsems[ename])
                waited = 0
                for si, st in enumerate(self.stages):
                    fns = st[ename]
                    if not fns:
                        continue
                    need = thr[si - 1] if si > 0 else 0
                    if need > 0:
                        eng.wait_ge(stg, need)
                    c.last = None
                    for fn in fns:
                        fn(c)
                    if c.ndma > waited:
                        eng.wait_ge(dsems[ename], 16 * c.ndma)
                        waited = c.ndma
                        eng.nop().then_inc(stg, 1)
                    elif c.last is not None:
                        eng.drain().then_inc(stg, 1)
                    else:
                        eng.nop().then_inc(stg, 1)
            getattr(block, ename)(body)


def alloc_common(nc, ctx):
    t = {}
    t["ones_col_b"] = ctx.enter_context(
        nc.sbuf_tensor("ones_col_b", [128, 1], BF16))
    t["ones_row_f"] = ctx.enter_context(
        nc.sbuf_tensor("ones_row_f", [1, 128], F32))
    return t


def load_common(sp, t, ones_col_d, ones_row_d):
    s = sp.stage("load_common")
    s.gpsimd(lambda c: c.dma(t["ones_col_b"][:], ones_col_d))
    s.sync(lambda c: c.dma(t["ones_row_f"][:], ones_row_d))


def alloc_ln(nc, ctx, t, pref, n):
    sb = lambda nm, sh, dt: ctx.enter_context(nc.sbuf_tensor(nm, sh, dt))
    t[pref + "_xb"] = sb(pref + "_xb", [128, KC * n], BF16)
    t[pref + "_xsq"] = sb(pref + "_xsq", [128, KC * n], BF16)
    t[pref + "_ps_s"] = ctx.enter_context(
        nc.psum_tensor(pref + "_ps_s", [1, n], F32))
    t[pref + "_ps_q"] = ctx.enter_context(
        nc.psum_tensor(pref + "_ps_q", [1, n], F32))
    for nm in ("rm", "rq", "rt", "rr"):
        t[pref + "_" + nm] = sb(pref + "_" + nm, [1, n], F32)
    t[pref + "_mb"] = sb(pref + "_mb", [128, n], F32)
    t[pref + "_rb"] = sb(pref + "_rb", [128, n], F32)
    t[pref + "_ps_bm"] = ctx.enter_context(
        nc.psum_tensor(pref + "_ps_bm", [128, n], F32))
    t[pref + "_ps_br"] = ctx.enter_context(
        nc.psum_tensor(pref + "_ps_br", [128, n], F32))


def emit_ln_stats(sp, nc, t, xin_chunks, n, pref):
    xb, xsq = t[pref + "_xb"], t[pref + "_xsq"]
    ps_s, ps_q = t[pref + "_ps_s"], t[pref + "_ps_q"]
    rm, rq, rt, rr = (t[pref + "_rm"], t[pref + "_rq"],
                      t[pref + "_rt"], t[pref + "_rr"])
    s = sp.stage(pref + "_cast")
    for i in range(KC):
        s.vector(lambda c, i=i: c.e.tensor_copy(
            xb[:, i * n:(i + 1) * n], xin_chunks[i]))
        s.scalar(lambda c, i=i: c.e.activation(
            xsq[:, i * n:(i + 1) * n], xin_chunks[i], AF.Square))
    s = sp.stage(pref + "_summ")
    def mm_sums(c):
        for i in range(KC):
            c.e.matmul(ps_s[:], t["ones_col_b"][:], xb[:, i * n:(i + 1) * n],
                       start=(i == 0), stop=(i == KC - 1))
        for i in range(KC):
            c.e.matmul(ps_q[:], t["ones_col_b"][:], xsq[:, i * n:(i + 1) * n],
                       start=(i == 0), stop=(i == KC - 1))
    s.tensor(mm_sums)
    s = sp.stage(pref + "_rows")
    def rows(c):
        v = c.e
        v.tensor_scalar_mul(rm[:], ps_s[:], 1.0 / D)
        v.tensor_scalar_mul(rq[:], ps_q[:], 1.0 / D)
        v.drain()
        v.tensor_mul(rt[:], rm[:], rm[:])
        v.drain()
        v.tensor_sub(rq[:], rq[:], rt[:])
        v.drain()
        v.tensor_scalar_add(rq[:], rq[:], 1e-5)
    s.vector(rows)
    s = sp.stage(pref + "_rstd")
    def rstd(c):
        sc = c.e
        sc.activation(rt[:], rq[:], AF.Ln)
        sc.drain()
        sc.activation(rr[:], rt[:], AF.Exp, scale=-0.5)
    s.scalar(rstd)
    s = sp.stage(pref + "_bcast")
    def bc(c):
        c.e.matmul(t[pref + "_ps_bm"][:], t["ones_row_f"][:], rm[:],
                   start=True, stop=True)
        c.e.matmul(t[pref + "_ps_br"][:], t["ones_row_f"][:], rr[:],
                   start=True, stop=True)
    s.tensor(bc)
    s = sp.stage(pref + "_bev")
    s.vector(lambda c: c.e.tensor_copy(t[pref + "_mb"][:],
                                       t[pref + "_ps_bm"][:]))
    s.vector(lambda c: c.e.tensor_copy(t[pref + "_rb"][:],
                                       t[pref + "_ps_br"][:]))


def emit_ln_apply(sp, name, xin, out, mb, rb, scr4, n):
    s = sp.stage(name)
    def go(c):
        v = c.e
        for r in range(4):
            for j in range(4):
                i = r * 4 + j
                v.tensor_sub(scr4[:, j * n:(j + 1) * n], xin[i], mb[:])
            v.drain()
            for j in range(4):
                i = r * 4 + j
                v.tensor_mul(out[i], scr4[:, j * n:(j + 1) * n], rb[:])
            v.drain()
    s.vector(go)


def emit_res_add(sp, name, x_chunks, ln_chunks):
    s = sp.stage(name)
    def go(c):
        v = c.e
        for i in range(KC):
            v.tensor_add(x_chunks[i], x_chunks[i], ln_chunks[i])
    s.vector(go)


def build_l0_front():
    nc = bass.Bass("TRN2", target_bir_lowering=False,
                   detect_race_conditions=DETECT_RACE)
    xT = nc.dram_tensor("xT", [D, TPC], F32, kind="ExternalInput").ap()
    prev = nc.dram_tensor("prev", [D, 1], F32, kind="ExternalInput").ap()
    wk = nc.dram_tensor("wkT", [D, D], F32, kind="ExternalInput").ap()
    wv = nc.dram_tensor("wvT", [D, D], F32, kind="ExternalInput").ap()
    wr = nc.dram_tensor("wrT", [D, D], F32, kind="ExternalInput").ap()
    tm = nc.dram_tensor("tm", [D, 3], F32, kind="ExternalInput").ap()
    ones_col_d = nc.dram_tensor("ones_col", [128, 1], F32,
                                kind="ExternalInput").ap()
    ones_row_d = nc.dram_tensor("ones_row", [1, 128], F32,
                                kind="ExternalInput").ap()
    ko = nc.dram_tensor("k_out", [D, TPC], F32, kind="ExternalOutput").ap()
    vo = nc.dram_tensor("v_out", [D, TPC], F32, kind="ExternalOutput").ap()
    so = nc.dram_tensor("sr_out", [D, TPC], F32, kind="ExternalOutput").ap()

    with ExitStack() as ctx:
        t = alloc_common(nc, ctx)
        sb = lambda nm, sh, dt=F32: ctx.enter_context(nc.sbuf_tensor(nm, sh, dt))
        psf = lambda nm, sh: ctx.enter_context(nc.psum_tensor(nm, sh, F32))
        xt = sb("xt", [128, KC * TT])
        alloc_ln(nc, ctx, t, "ln", TT)
        xln = sb("xln", [128, KC * TT], BF16)
        lxcol = sb("lxcol", [128, KC], BF16)
        prevt = sb("prevt", [128, KC])
        prevb = sb("prevb", [128, KC], BF16)
        prevq = sb("prevq", [128, KC], BF16)
        tmt = sb("tmt", [128, KC * 3])
        dmx = sb("dmx", [128, 4 * TT], BF16)
        mix = sb("mix", [128, 3 * KC * TT], BF16)
        wb = sb("wb", [128, KC * 512], BF16)
        evac = sb("evac", [128, 4 * TT])
        ps_o = [psf(f"ps_o{i}", [128, TT]) for i in range(4)]
        sp = SP(nc)
        load_common(sp, t, ones_col_d, ones_row_d)
        s = sp.stage("tm_load")
        s.sync(lambda c: c.dma(
            tmt[:].rearrange("p (kc three) -> p kc three", three=3),
            tm.rearrange("(kc p) three -> p kc three", p=128), slow=True))
        s.sync(lambda c: c.dma(
            prevt[:].rearrange("p (kc one) -> p kc one", one=1),
            prev.rearrange("(kc p) one -> p kc one", p=128), slow=True))
        s = sp.stage("pv_cast")
        s.vector(lambda c: c.e.tensor_copy(prevb[:], prevt[:]))
        s.scalar(lambda c: c.e.activation(prevq[:], prevt[:], AF.Square))
        s = sp.stage("pv_summ")
        def pv_mm(c):
            c.e.matmul(t["ln_ps_s"][:, 0:KC], t["ones_col_b"][:], prevb[:],
                       start=True, stop=True)
            c.e.matmul(t["ln_ps_q"][:, 0:KC], t["ones_col_b"][:], prevq[:],
                       start=True, stop=True)
        s.tensor(pv_mm)
        s = sp.stage("pv_rows")
        def pv_rows(c):
            v = c.e
            v.reduce_sum(t["ln_rm"][:, 0:1], t["ln_ps_s"][:, 0:KC], axis=AX)
            v.reduce_sum(t["ln_rq"][:, 0:1], t["ln_ps_q"][:, 0:KC], axis=AX)
            v.drain()
            v.tensor_scalar_mul(t["ln_rm"][:, 0:1], t["ln_rm"][:, 0:1], 1.0 / D)
            v.tensor_scalar_mul(t["ln_rq"][:, 0:1], t["ln_rq"][:, 0:1], 1.0 / D)
            v.drain()
            v.tensor_mul(t["ln_rt"][:, 0:1], t["ln_rm"][:, 0:1],
                         t["ln_rm"][:, 0:1])
            v.drain()
            v.tensor_sub(t["ln_rq"][:, 0:1], t["ln_rq"][:, 0:1],
                         t["ln_rt"][:, 0:1])
            v.drain()
            v.tensor_scalar_add(t["ln_rq"][:, 0:1], t["ln_rq"][:, 0:1], 1e-5)
        s.vector(pv_rows)
        s = sp.stage("pv_rstd")
        def pv_rstd(c):
            sc = c.e
            sc.activation(t["ln_rt"][:, 0:1], t["ln_rq"][:, 0:1], AF.Ln)
            sc.drain()
            sc.activation(t["ln_rr"][:, 0:1], t["ln_rt"][:, 0:1], AF.Exp,
                          scale=-0.5)
        s.scalar(pv_rstd)
        s = sp.stage("pv_bc")
        def pv_bc(c):
            c.e.matmul(t["ln_ps_bm"][:, 0:1], t["ones_row_f"][:],
                       t["ln_rm"][:, 0:1], start=True, stop=True)
            c.e.matmul(t["ln_ps_br"][:, 0:1], t["ones_row_f"][:],
                       t["ln_rr"][:, 0:1], start=True, stop=True)
        s.tensor(pv_bc)
        s = sp.stage("pv_apply")
        def pv_ap(c):
            v = c.e
            v.tensor_copy(t["ln_mb"][:, 0:1], t["ln_ps_bm"][:, 0:1])
            v.tensor_copy(t["ln_rb"][:, 0:1], t["ln_ps_br"][:, 0:1])
            v.drain()
            v.tensor_scalar(lxcol[:], prevt[:], t["ln_mb"][:, 0:1],
                            t["ln_rb"][:, 0:1], OP.subtract, OP.mult)
        s.vector(pv_ap)

        for tile in range(NTILES):
            off = tile * TT
            s = sp.stage(f"x_load{tile}")
            s.sync(lambda c, off=off: c.dma(
                xt[:].rearrange("p (kc n) -> p kc n", kc=KC),
                xT[:, off:off + TT].rearrange("(kc p) n -> p kc n", p=128)))
            xin = [xt[:, i * TT:(i + 1) * TT] for i in range(KC)]
            emit_ln_stats(sp, nc, t, xin, TT, "ln")
            emit_ln_apply(sp, f"xln{tile}", xin,
                          [xln[:, i * TT:(i + 1) * TT] for i in range(KC)],
                          t["ln_mb"], t["ln_rb"], evac, TT)
            s = sp.stage(f"mix{tile}")
            def mx_all(c):
                v = c.e
                for r in range(4):
                    for j in range(4):
                        i = r * 4 + j
                        base = i * TT
                        v.tensor_sub(dmx[:, j * TT + 1:(j + 1) * TT],
                                     xln[:, base + 1:base + TT],
                                     xln[:, base:base + TT - 1])
                        v.tensor_sub(dmx[:, j * TT:j * TT + 1],
                                     xln[:, base:base + 1], lxcol[:, i:i + 1])
                    v.drain()
                    for j in range(4):
                        i = r * 4 + j
                        base = i * TT
                        lx1 = xln[:, base:base + TT - 1]
                        lx0 = lxcol[:, i:i + 1]
                        for jj in range(3):
                            mslot = mix[:, (jj * KC + i) * TT:
                                        (jj * KC + i + 1) * TT]
                            v.scalar_tensor_tensor(
                                mslot[:, 1:TT], dmx[:, j * TT + 1:(j + 1) * TT],
                                tmt[:, i * 3 + jj:i * 3 + jj + 1], lx1,
                                OP.mult, OP.add)
                            v.scalar_tensor_tensor(
                                mslot[:, 0:1], dmx[:, j * TT:j * TT + 1],
                                tmt[:, i * 3 + jj:i * 3 + jj + 1], lx0,
                                OP.mult, OP.add)
                    v.drain()
            s.vector(mx_all)
            s = sp.stage(f"lxsave{tile}")
            def lxs(c):
                v = c.e
                for i in range(KC):
                    v.tensor_copy(lxcol[:, i:i + 1],
                                  xln[:, i * TT + TT - 1:i * TT + TT])
            s.vector(lxs)
            for pj, (wd, outd, isr) in enumerate(
                    [(wk, ko, False), (wv, vo, False), (wr, so, True)]):
                for ob in range(4):
                    s = sp.stage(f"w{tile}_{pj}_{ob}")
                    s.gpsimd(lambda c, wd=wd, ob=ob: c.dma(
                        wb[:].rearrange("p (kc o) -> p kc o", o=512),
                        wd[:, ob * 512:(ob + 1) * 512]
                        .rearrange("(kc p) o -> p kc o", p=128)))
                    s = sp.stage(f"mm{tile}_{pj}_{ob}")
                    def mms(c, pj=pj):
                        for oc in range(4):
                            for k in range(KC):
                                c.e.matmul(
                                    ps_o[oc][:],
                                    wb[:, k * 512 + oc * 128:
                                        k * 512 + (oc + 1) * 128],
                                    mix[:, (pj * KC + k) * TT:
                                        (pj * KC + k + 1) * TT],
                                    start=(k == 0), stop=(k == KC - 1))
                    s.tensor(mms)
                    s = sp.stage(f"ev{tile}_{pj}_{ob}")
                    for oc in range(4):
                        if isr:
                            s.scalar(lambda c, oc=oc: c.e.activation(
                                evac[:, oc * TT:(oc + 1) * TT], ps_o[oc][:],
                                AF.Sigmoid))
                        else:
                            s.vector(lambda c, oc=oc: c.e.tensor_copy(
                                evac[:, oc * TT:(oc + 1) * TT], ps_o[oc][:]))
                    s = sp.stage(f"st{tile}_{pj}_{ob}")
                    s.sync(lambda c, outd=outd, ob=ob, off=off: c.dma(
                        outd[ob * 512:(ob + 1) * 512, off:off + TT]
                        .rearrange("(oc p) n -> p oc n", p=128),
                        evac[:].rearrange("p (oc n) -> p oc n", oc=4)))
        sp.emit(ctx)
    return nc


def build_l0_scan():
    CH, T = 1024, 4096
    G = CH // 128
    NT = T // TT
    nc = bass.Bass("TRN2", target_bir_lowering=False,
                   detect_race_conditions=DETECT_RACE)
    kd = nc.dram_tensor("k_ch", [CH, T], F32, kind="ExternalInput").ap()
    vd = nc.dram_tensor("v_ch", [CH, T], F32, kind="ExternalInput").ap()
    td = nc.dram_tensor("td", [CH, 1], F32, kind="ExternalInput").ap()
    tf = nc.dram_tensor("tf", [CH, 1], F32, kind="ExternalInput").ap()
    wo = nc.dram_tensor("wkv_out", [CH, T], F32, kind="ExternalOutput").ap()

    with ExitStack() as ctx:
        sb = lambda nm, sh, dt=F32: ctx.enter_context(nc.sbuf_tensor(nm, sh, dt))
        kt = sb("kt", [128, G * TT])
        vt = sb("vt", [128, G * TT])
        ekm = sb("ekm", [128, G * TT])
        pt = sb("pt", [128, G * TT])
        numt = sb("numt", [128, G * TT])
        dent = sb("dent", [128, G * TT])
        wkvt = sb("wkvt", [128, G * TT])
        abuf = sb("abuf", [128, G * (TT + 1)])
        bbuf = sb("bbuf", [128, G * (TT + 1)])
        dect = sb("dect", [128, G * TT])
        onesf = sb("onesf", [128, TT])
        cols = sb("cols", [128, 8 * G])
        Mprev = sb("Mprev", [128, G])
        sp = SP(nc)

        s = sp.stage("load_cols")
        s.sync(lambda c: c.dma(cols[:, 0:G].rearrange("p (g one) -> p g one", one=1),
                               td.rearrange("(g p) one -> p g one", p=128), slow=True))
        s.sync(lambda c: c.dma(cols[:, G:2 * G].rearrange("p (g one) -> p g one", one=1),
                               tf.rearrange("(g p) one -> p g one", p=128), slow=True))
        s = sp.stage("prep")
        def prep_s(c):
            sc = c.e
            sc.activation(cols[:, 2 * G:3 * G], cols[:, 0:G], AF.Exp)
            sc.activation(cols[:, 4 * G:5 * G], cols[:, G:2 * G], AF.Exp)
        s.scalar(prep_s)
        s = sp.stage("prep2")
        s.scalar(lambda c: c.e.activation(cols[:, 3 * G:4 * G],
                                          cols[:, 2 * G:3 * G], AF.Exp,
                                          scale=-1.0))
        s = sp.stage("prep3")
        def prep_v(c):
            v = c.e
            v.memset(onesf[:], 1.0)
            v.drain()
            for g in range(G):
                v.tensor_scalar_mul(dect[:, g * TT:(g + 1) * TT], onesf[:],
                                    cols[:, 3 * G + g:3 * G + g + 1])
        s.vector(prep_v)

        for ti in range(NT):
            off = ti * TT
            s = sp.stage(f"ld{ti}")
            s.sync(lambda c, off=off: c.dma(
                kt[:].rearrange("p (g n) -> p g n", g=G),
                kd[:, off:off + TT].rearrange("(g p) n -> p g n", p=128)))
            s.sync(lambda c, off=off: c.dma(
                vt[:].rearrange("p (g n) -> p g n", g=G),
                vd[:, off:off + TT].rearrange("(g p) n -> p g n", p=128)))
            s = sp.stage(f"max{ti}")
            def mx(c, ti=ti):
                v = c.e
                v.reduce_max(cols[:, 5 * G:6 * G],
                             kt[:].rearrange("p (g n) -> p g n", g=G), axis=AX)
                v.drain()
                v.tensor_scalar_mul(cols[:, 6 * G:7 * G],
                                    cols[:, 5 * G:6 * G], -1.0)
                if ti > 0:
                    v.tensor_sub(cols[:, 7 * G:8 * G], Mprev[:],
                                 cols[:, 5 * G:6 * G])
                    v.drain()
                v.tensor_copy(Mprev[:], cols[:, 5 * G:6 * G])
            s.vector(mx)
            s = sp.stage(f"exp{ti}")
            def ex(c, ti=ti):
                sc = c.e
                for g in range(G):
                    sc.activation(ekm[:, g * TT:(g + 1) * TT],
                                  kt[:, g * TT:(g + 1) * TT], AF.Exp,
                                  bias=cols[:, 6 * G + g:6 * G + g + 1])
                if ti > 0:
                    sc.activation(cols[:, 7 * G:8 * G], cols[:, 7 * G:8 * G],
                                  AF.Exp)
            s.scalar(ex)
            s = sp.stage(f"scan{ti}")
            def scan(c, ti=ti):
                v = c.e
                v.tensor_mul(pt[:], ekm[:], vt[:])
                for g in range(G):
                    a0 = abuf[:, g * (TT + 1):g * (TT + 1) + 1]
                    b0 = bbuf[:, g * (TT + 1):g * (TT + 1) + 1]
                    if ti == 0:
                        v.memset(a0, 0.0)
                        v.memset(b0, 0.0)
                    else:
                        v.tensor_mul(a0, abuf[:, g * (TT + 1) + TT:
                                              g * (TT + 1) + TT + 1],
                                     cols[:, 7 * G + g:7 * G + g + 1])
                        v.tensor_mul(b0, bbuf[:, g * (TT + 1) + TT:
                                              g * (TT + 1) + TT + 1],
                                     cols[:, 7 * G + g:7 * G + g + 1])
                v.drain()
                for g in range(G):
                    sl = slice(g * (TT + 1) + 1, g * (TT + 1) + TT + 1)
                    v.tensor_tensor_scan(
                        abuf[:, sl], dect[:, g * TT:(g + 1) * TT],
                        pt[:, g * TT:(g + 1) * TT],
                        abuf[:, g * (TT + 1):g * (TT + 1) + 1],
                        OP.mult, OP.add)
                    v.tensor_tensor_scan(
                        bbuf[:, sl], dect[:, g * TT:(g + 1) * TT],
                        ekm[:, g * TT:(g + 1) * TT],
                        bbuf[:, g * (TT + 1):g * (TT + 1) + 1],
                        OP.mult, OP.add)
                v.drain()
                for g in range(G):
                    pv = slice(g * (TT + 1), g * (TT + 1) + TT)
                    v.scalar_tensor_tensor(
                        numt[:, g * TT:(g + 1) * TT],
                        pt[:, g * TT:(g + 1) * TT],
                        cols[:, 4 * G + g:4 * G + g + 1], abuf[:, pv],
                        OP.mult, OP.add)
                    v.scalar_tensor_tensor(
                        dent[:, g * TT:(g + 1) * TT],
                        ekm[:, g * TT:(g + 1) * TT],
                        cols[:, 4 * G + g:4 * G + g + 1], bbuf[:, pv],
                        OP.mult, OP.add)
            s.vector(scan)
            s = sp.stage(f"div{ti}")
            def dv(c):
                sc = c.e
                sc.activation(dent[:], dent[:], AF.Ln)
                sc.drain()
                sc.activation(dent[:], dent[:], AF.Exp, scale=-1.0)
            s.scalar(dv)
            s = sp.stage(f"wkv{ti}")
            s.vector(lambda c: c.e.tensor_mul(wkvt[:], numt[:], dent[:]))
            s = sp.stage(f"stout{ti}")
            s.sync(lambda c, off=off: c.dma(
                wo[:, off:off + TT].rearrange("(g p) n -> p g n", p=128),
                wkvt[:].rearrange("p (g n) -> p g n", g=G)))
        sp.emit(ctx)
    return nc


def build_l0_back():
    nc = bass.Bass("TRN2", target_bir_lowering=False,
                   detect_race_conditions=DETECT_RACE)
    xT = nc.dram_tensor("xT", [D, TPC], F32, kind="ExternalInput").ap()
    wkvT = nc.dram_tensor("wkvT", [D, TPC], F32, kind="ExternalInput").ap()
    srT = nc.dram_tensor("srT", [D, TPC], F32, kind="ExternalInput").ap()
    woT = nc.dram_tensor("woT", [D, D], F32, kind="ExternalInput").ap()
    wgT = nc.dram_tensor("wgT", [D, HID], F32, kind="ExternalInput").ap()
    wuT = nc.dram_tensor("wuT", [D, HID], F32, kind="ExternalInput").ap()
    wdT = nc.dram_tensor("wdT", [HID, D], F32, kind="ExternalInput").ap()
    ones_col_d = nc.dram_tensor("ones_col", [128, 1], F32,
                                kind="ExternalInput").ap()
    ones_row_d = nc.dram_tensor("ones_row", [1, 128], F32,
                                kind="ExternalInput").ap()
    x2o = nc.dram_tensor("x2_out", [D, TPC], F32, kind="ExternalOutput").ap()

    with ExitStack() as ctx:
        t = alloc_common(nc, ctx)
        sb = lambda nm, sh, dt=F32: ctx.enter_context(nc.sbuf_tensor(nm, sh, dt))
        psf = lambda nm, sh: ctx.enter_context(nc.psum_tensor(nm, sh, F32))
        xt = sb("xt", [128, KC * TT])
        ao = sb("ao", [128, KC * TT])
        zt = sb("zt", [128, KC * TT], BF16)
        scr = sb("scr", [128, 4 * TT])
        act_q = sb("act_q", [128, 16 * TT], BF16)
        wb = sb("wb", [128, KC * 512], BF16)
        wb2 = sb("wb2", [128, KC * 512], BF16)
        wdb = sb("wdb", [128, 16 * 512], BF16)
        evac = sb("evac", [128, 4 * TT])
        alloc_ln(nc, ctx, t, "ln", TT)
        ps_o = [psf(f"ps_o{i}", [128, TT]) for i in range(4)]
        sp = SP(nc)
        load_common(sp, t, ones_col_d, ones_row_d)

        for tile in range(NTILES):
            off = tile * TT
            s = sp.stage(f"zld{tile}")
            s.sync(lambda c, off=off: c.dma(
                xt[:].rearrange("p (kc n) -> p kc n", kc=KC),
                wkvT[:, off:off + TT].rearrange("(kc p) n -> p kc n", p=128)))
            s.sync(lambda c, off=off: c.dma(
                ao[:].rearrange("p (kc n) -> p kc n", kc=KC),
                srT[:, off:off + TT].rearrange("(kc p) n -> p kc n", p=128)))
            s = sp.stage(f"zmul{tile}")
            s.vector(lambda c: c.e.tensor_mul(zt[:], xt[:], ao[:]))
            for ob in range(4):
                s = sp.stage(f"wo{tile}_{ob}")
                s.gpsimd(lambda c, ob=ob: c.dma(
                    wb[:].rearrange("p (kc o) -> p kc o", o=512),
                    woT[:, ob * 512:(ob + 1) * 512]
                    .rearrange("(kc p) o -> p kc o", p=128)))
                s = sp.stage(f"womm{tile}_{ob}")
                def mms(c):
                    for oc in range(4):
                        for k in range(KC):
                            c.e.matmul(
                                ps_o[oc][:],
                                wb[:, k * 512 + oc * 128:k * 512 + (oc + 1) * 128],
                                zt[:, k * TT:(k + 1) * TT],
                                start=(k == 0), stop=(k == KC - 1))
                s.tensor(mms)
                s = sp.stage(f"woev{tile}_{ob}")
                for oc in range(4):
                    s.vector(lambda c, ob=ob, oc=oc: c.e.tensor_copy(
                        ao[:, (ob * 4 + oc) * TT:(ob * 4 + oc + 1) * TT],
                        ps_o[oc][:]))
            aoc = [ao[:, i * TT:(i + 1) * TT] for i in range(KC)]
            emit_ln_stats(sp, nc, t, aoc, TT, "ln")
            s = sp.stage(f"xld{tile}")
            s.sync(lambda c, off=off: c.dma(
                xt[:].rearrange("p (kc n) -> p kc n", kc=KC),
                xT[:, off:off + TT].rearrange("(kc p) n -> p kc n", p=128)))
            aoc_l = [ao[:, i * TT:(i + 1) * TT] for i in range(KC)]
            emit_ln_apply(sp, f"lna1_{tile}", aoc_l, aoc_l,
                          t["ln_mb"], t["ln_rb"], scr, TT)
            emit_res_add(sp, f"res1_{tile}",
                         [xt[:, i * TT:(i + 1) * TT] for i in range(KC)],
                         aoc_l)
            xc = [xt[:, i * TT:(i + 1) * TT] for i in range(KC)]
            emit_ln_stats(sp, nc, t, xc, TT, "ln")
            emit_ln_apply(sp, f"xln2_{tile}", xc,
                          [zt[:, i * TT:(i + 1) * TT] for i in range(KC)],
                          t["ln_mb"], t["ln_rb"], scr, TT)
            for q in range(4):
                for hb in range(4):
                    h0 = q * 2048 + hb * 512
                    s = sp.stage(f"fw{tile}_{q}_{hb}")
                    s.gpsimd(lambda c, h0=h0: c.dma(
                        wb[:].rearrange("p (kc o) -> p kc o", o=512),
                        wgT[:, h0:h0 + 512]
                        .rearrange("(kc p) o -> p kc o", p=128)))
                    s.gpsimd(lambda c, h0=h0: c.dma(
                        wb2[:].rearrange("p (kc o) -> p kc o", o=512),
                        wuT[:, h0:h0 + 512]
                        .rearrange("(kc p) o -> p kc o", p=128)))
                    s = sp.stage(f"fg{tile}_{q}_{hb}")
                    def gmm(c):
                        for oc in range(4):
                            for k in range(KC):
                                c.e.matmul(
                                    ps_o[oc][:],
                                    wb[:, k * 512 + oc * 128:
                                        k * 512 + (oc + 1) * 128],
                                    zt[:, k * TT:(k + 1) * TT],
                                    start=(k == 0), stop=(k == KC - 1))
                    s.tensor(gmm)
                    s = sp.stage(f"fsilu{tile}_{q}_{hb}")
                    for oc in range(4):
                        s.scalar(lambda c, oc=oc: c.e.activation(
                            evac[:, oc * TT:(oc + 1) * TT], ps_o[oc][:],
                            AF.Silu))
                    s = sp.stage(f"fu{tile}_{q}_{hb}")
                    def umm(c):
                        for oc in range(4):
                            for k in range(KC):
                                c.e.matmul(
                                    ps_o[oc][:],
                                    wb2[:, k * 512 + oc * 128:
                                         k * 512 + (oc + 1) * 128],
                                    zt[:, k * TT:(k + 1) * TT],
                                    start=(k == 0), stop=(k == KC - 1))
                    s.tensor(umm)
                    s = sp.stage(f"fact{tile}_{q}_{hb}")
                    for oc in range(4):
                        s.vector(lambda c, hb=hb, oc=oc: c.e.tensor_mul(
                            act_q[:, (hb * 4 + oc) * TT:(hb * 4 + oc + 1) * TT],
                            evac[:, oc * TT:(oc + 1) * TT], ps_o[oc][:]))
                for db in range(4):
                    s = sp.stage(f"fdw{tile}_{q}_{db}")
                    s.gpsimd(lambda c, q=q, db=db: c.dma(
                        wdb[:].rearrange("p (hc o) -> p hc o", o=512),
                        wdT[q * 2048:(q + 1) * 2048, db * 512:(db + 1) * 512]
                        .rearrange("(hc p) o -> p hc o", p=128)))
                    s = sp.stage(f"fdm{tile}_{q}_{db}")
                    def dmm(c):
                        for oc in range(4):
                            for k in range(16):
                                c.e.matmul(
                                    ps_o[oc][:],
                                    wdb[:, k * 512 + oc * 128:
                                         k * 512 + (oc + 1) * 128],
                                    act_q[:, k * TT:(k + 1) * TT],
                                    start=(k == 0), stop=(k == 15))
                    s.tensor(dmm)
                    s = sp.stage(f"fde{tile}_{q}_{db}")
                    for oc in range(4):
                        if q == 0:
                            s.vector(lambda c, db=db, oc=oc: c.e.tensor_copy(
                                ao[:, (db * 4 + oc) * TT:(db * 4 + oc + 1) * TT],
                                ps_o[oc][:]))
                        else:
                            s.vector(lambda c, db=db, oc=oc: c.e.tensor_add(
                                ao[:, (db * 4 + oc) * TT:(db * 4 + oc + 1) * TT],
                                ao[:, (db * 4 + oc) * TT:(db * 4 + oc + 1) * TT],
                                ps_o[oc][:]))
            fc = [ao[:, i * TT:(i + 1) * TT] for i in range(KC)]
            emit_ln_stats(sp, nc, t, fc, TT, "ln")
            emit_ln_apply(sp, f"lna2_{tile}", fc, fc,
                          t["ln_mb"], t["ln_rb"], scr, TT)
            emit_res_add(sp, f"res2_{tile}",
                         [xt[:, i * TT:(i + 1) * TT] for i in range(KC)], fc)
            s = sp.stage(f"x2st{tile}")
            s.sync(lambda c, off=off: c.dma(
                x2o[:, off:off + TT].rearrange("(kc p) n -> p kc n", p=128),
                xt[:].rearrange("p (kc n) -> p kc n", kc=KC)))
        sp.emit(ctx)
    return nc


def build_l1_qkv():
    nc = bass.Bass("TRN2", target_bir_lowering=False,
                   detect_race_conditions=DETECT_RACE)
    xT = nc.dram_tensor("xT", [D, TPC], F32, kind="ExternalInput").ap()
    wq = nc.dram_tensor("wqkvT", [D, 3 * D], F32, kind="ExternalInput").ap()
    cosd = nc.dram_tensor("cosd", [128, TPC], F32, kind="ExternalInput").ap()
    sind = nc.dram_tensor("sind", [128, TPC], F32, kind="ExternalInput").ap()
    rotd = nc.dram_tensor("rotP", [128, 128], F32, kind="ExternalInput").ap()
    ones_col_d = nc.dram_tensor("ones_col", [128, 1], F32,
                                kind="ExternalInput").ap()
    ones_row_d = nc.dram_tensor("ones_row", [1, 128], F32,
                                kind="ExternalInput").ap()
    qo = nc.dram_tensor("q_out", [D, TPC], BF16, kind="ExternalOutput").ap()
    ko = nc.dram_tensor("k_out", [D, TPC], BF16, kind="ExternalOutput").ap()
    vo = nc.dram_tensor("v_out", [D, TPC], BF16, kind="ExternalOutput").ap()

    with ExitStack() as ctx:
        t = alloc_common(nc, ctx)
        sb = lambda nm, sh, dt=F32: ctx.enter_context(nc.sbuf_tensor(nm, sh, dt))
        psf = lambda nm, sh, dt=F32: ctx.enter_context(
            nc.psum_tensor(nm, sh, dt))
        xt = sb("xt", [128, KC * TT])
        xln = sb("xln", [128, KC * TT], BF16)
        scr = sb("scr", [128, 4 * TT])
        emin = sb("emin", [128, 4 * TT])
        eexp = sb("eexp", [128, 4 * TT])
        cost = sb("cost", [128, TT])
        sint = sb("sint", [128, TT])
        rott = sb("rott", [128, 128], BF16)
        wb = sb("wb", [128, KC * 512], BF16)
        qkb = sb("qkb", [128, 4 * TT], BF16)
        ebuf = sb("ebuf", [128, 4 * TT])
        outb = sb("outb", [128, 4 * TT], BF16)
        alloc_ln(nc, ctx, t, "ln", TT)
        ps_o = [psf(f"ps_o{i}", [128, TT]) for i in range(4)]
        sp = SP(nc)
        load_common(sp, t, ones_col_d, ones_row_d)
        s = sp.stage("rot_load")
        s.gpsimd(lambda c: c.dma(rott[:], rotd))

        for tile in range(NTILES):
            off = tile * TT
            s = sp.stage(f"xld{tile}")
            s.sync(lambda c, off=off: c.dma(
                xt[:].rearrange("p (kc n) -> p kc n", kc=KC),
                xT[:, off:off + TT].rearrange("(kc p) n -> p kc n", p=128)))
            s.sync(lambda c, off=off: c.dma(cost[:], cosd[:, off:off + TT]))
            s.sync(lambda c, off=off: c.dma(sint[:], sind[:, off:off + TT]))
            xc = [xt[:, i * TT:(i + 1) * TT] for i in range(KC)]
            emit_ln_stats(sp, nc, t, xc, TT, "ln")
            emit_ln_apply(sp, f"xln{tile}", xc,
                          [xln[:, i * TT:(i + 1) * TT] for i in range(KC)],
                          t["ln_mb"], t["ln_rb"], scr, TT)
            for ob in range(12):
                kind = ob // 4
                s = sp.stage(f"w{tile}_{ob}")
                s.gpsimd(lambda c, ob=ob: c.dma(
                    wb[:].rearrange("p (kc o) -> p kc o", o=512),
                    wq[:, ob * 512:(ob + 1) * 512]
                    .rearrange("(kc p) o -> p kc o", p=128)))
                s = sp.stage(f"mm{tile}_{ob}")
                def mms(c):
                    for oc in range(4):
                        for k in range(KC):
                            c.e.matmul(
                                ps_o[oc][:],
                                wb[:, k * 512 + oc * 128:k * 512 + (oc + 1) * 128],
                                xln[:, k * TT:(k + 1) * TT],
                                start=(k == 0), stop=(k == KC - 1))
                s.tensor(mms)
                if kind == 2:
                    s = sp.stage(f"vev{tile}_{ob}")
                    for oc in range(4):
                        s.vector(lambda c, oc=oc: c.e.tensor_copy(
                            outb[:, oc * TT:(oc + 1) * TT], ps_o[oc][:]))
                    s = sp.stage(f"vst{tile}_{ob}")
                    s.sync(lambda c, ob=ob, off=off: c.dma(
                        vo[(ob - 8) * 512:(ob - 7) * 512, off:off + TT]
                        .rearrange("(oc p) n -> p oc n", p=128),
                        outb[:].rearrange("p (oc n) -> p oc n", oc=4)))
                else:
                    s = sp.stage(f"qe{tile}_{ob}")
                    for oc in range(4):
                        s.vector(lambda c, oc=oc: c.e.tensor_copy(
                            ebuf[:, oc * TT:(oc + 1) * TT], ps_o[oc][:]))
                    s = sp.stage(f"qb{tile}_{ob}")
                    s.vector(lambda c: c.e.tensor_copy(qkb[:], ebuf[:]))
                    s = sp.stage(f"rot{tile}_{ob}")
                    def rmm(c):
                        for oc in range(4):
                            c.e.matmul(ps_o[oc][:], rott[:],
                                       qkb[:, oc * TT:(oc + 1) * TT],
                                       start=True, stop=True)
                    s.tensor(rmm)
                    s = sp.stage(f"rope{tile}_{ob}")
                    def rope(c):
                        v = c.e
                        for pair in range(2):
                            o0, o1 = 2 * pair, 2 * pair + 1
                            s0 = slice(o0 * TT, (o0 + 1) * TT)
                            s1 = slice(o1 * TT, (o1 + 1) * TT)
                            v.tensor_mul(scr[:, 0:TT], ebuf[:, s0], cost[:])
                            v.tensor_mul(scr[:, TT:2 * TT], ps_o[o0][:],
                                         sint[:])
                            v.tensor_mul(scr[:, 2 * TT:3 * TT], ebuf[:, s1],
                                         cost[:])
                            v.tensor_mul(scr[:, 3 * TT:4 * TT], ps_o[o1][:],
                                         sint[:])
                            v.drain()
                            v.tensor_add(ebuf[:, s0], scr[:, 0:TT],
                                         scr[:, TT:2 * TT])
                            v.tensor_add(ebuf[:, s1], scr[:, 2 * TT:3 * TT],
                                         scr[:, 3 * TT:4 * TT])
                            v.drain()
                            v.tensor_scalar_min(emin[:, s0], ebuf[:, s0], 0.0)
                            v.tensor_scalar_max(ebuf[:, s0], ebuf[:, s0], 0.0)
                            v.tensor_scalar_min(emin[:, s1], ebuf[:, s1], 0.0)
                            v.tensor_scalar_max(ebuf[:, s1], ebuf[:, s1], 0.0)
                    s.vector(rope)
                    s = sp.stage(f"elu{tile}_{ob}")
                    for oc in range(4):
                        s.scalar(lambda c, oc=oc: c.e.activation(
                            eexp[:, oc * TT:(oc + 1) * TT],
                            emin[:, oc * TT:(oc + 1) * TT], AF.Exp))
                    s = sp.stage(f"elu2{tile}_{ob}")
                    for oc in range(4):
                        s.vector(lambda c, oc=oc: c.e.tensor_add(
                            outb[:, oc * TT:(oc + 1) * TT],
                            ebuf[:, oc * TT:(oc + 1) * TT],
                            eexp[:, oc * TT:(oc + 1) * TT]))
                    s = sp.stage(f"qst{tile}_{ob}")
                    outd = qo if kind == 0 else ko
                    bb = ob - 4 * kind
                    s.sync(lambda c, outd=outd, bb=bb, off=off: c.dma(
                        outd[bb * 512:(bb + 1) * 512, off:off + TT]
                        .rearrange("(oc p) n -> p oc n", p=128),
                        outb[:].rearrange("p (oc n) -> p oc n", oc=4)))
        sp.emit(ctx)
    return nc


def build_l1_attn():
    T = 4096
    NCH = T // CHUNK
    HPC = 8
    nc = bass.Bass("TRN2", target_bir_lowering=False,
                   detect_race_conditions=DETECT_RACE)
    qd = nc.dram_tensor("q_in", [HPC * HDIM, T], BF16, kind="ExternalInput").ap()
    kd = nc.dram_tensor("k_in", [HPC * HDIM, T], BF16, kind="ExternalInput").ap()
    vd = nc.dram_tensor("v_in", [HPC * HDIM, T], BF16, kind="ExternalInput").ap()
    trild = nc.dram_tensor("trilT", [CHUNK, CHUNK], F32,
                           kind="ExternalInput").ap()
    identd = nc.dram_tensor("ident", [128, 128], F32,
                            kind="ExternalInput").ap()
    ones_row_d = nc.dram_tensor("ones_row", [1, 128], F32,
                                kind="ExternalInput").ap()
    oo = nc.dram_tensor("ho_out", [HPC * HDIM, T], BF16,
                        kind="ExternalOutput").ap()

    with ExitStack() as ctx:
        sb = lambda nm, sh, dt=F32: ctx.enter_context(nc.sbuf_tensor(nm, sh, dt))
        psf = lambda nm, sh, dt=F32: ctx.enter_context(
            nc.psum_tensor(nm, sh, dt))
        qt = sb("qt", [128, T], BF16)
        kt = sb("kt", [128, T], BF16)
        vt = sb("vt", [128, T], BF16)
        trilt = sb("trilt", [128, 2 * CHUNK])
        identt = sb("identt", [128, 128], BF16)
        onesr = sb("onesr", [1, 128], F32)
        onesb = sb("onesb", [128, 1], BF16)
        attb = sb("attb", [128, 2 * CHUNK], BF16)
        tpb = sb("tpb", [128, 4 * 128], BF16)
        Ssb = sb("Ssb", [128, HDIM])
        Sb16 = sb("Sb16", [128, HDIM], BF16)
        Zsb = sb("Zsb", [128, 1])
        Zb16 = sb("Zb16", [128, 1], BF16)
        zscr = sb("zscr", [128, 1])
        numsb = sb("numsb", [128, T])
        densb = sb("densb", [1, T])
        outhb = sb("outhb", [128, T], BF16)
        ps_att = psf("ps_att", [128, 2 * CHUNK])
        ps_tp = psf("ps_tp", [128, 4 * 128], BF16)
        ps_num = psf("ps_num", [128, CHUNK])
        ps_den = psf("ps_den", [1, CHUNK])
        ps_S = psf("ps_S", [128, HDIM])
        ps_rec = psf("ps_rec", [128, TT])
        sp = SP(nc)
        s = sp.stage("const")
        s.sync(lambda c: c.dma(trilt[:].rearrange("p (mc l) -> p mc l", mc=2),
                               trild.rearrange("(mc p) l -> p mc l", p=128)))
        s.sync(lambda c: c.dma(onesr[:], ones_row_d))
        s.gpsimd(lambda c: c.dma(identt[:], identd))
        s.gpsimd(lambda c: c.dma(onesb[:], ones_row_d.rearrange("one p -> p one"), slow=True))

        for h in range(HPC):
            s = sp.stage(f"hld{h}")
            s.sync(lambda c, h=h: c.dma(qt[:], qd[h * 128:(h + 1) * 128, :]))
            s.sync(lambda c, h=h: c.dma(kt[:], kd[h * 128:(h + 1) * 128, :]))
            s.sync(lambda c, h=h: c.dma(vt[:], vd[h * 128:(h + 1) * 128, :]))
            def init(c):
                v = c.e
                v.memset(Ssb[:], 0.0)
                v.memset(Sb16[:], 0.0)
                v.memset(Zsb[:], 0.0)
                v.memset(Zb16[:], 0.0)
            s.vector(init)
            for ch in range(NCH):
                c0 = ch * CHUNK
                qc = qt[:, c0:c0 + CHUNK]
                kc = kt[:, c0:c0 + CHUNK]
                vc = vt[:, c0:c0 + CHUNK]
                s = sp.stage(f"att{h}_{ch}")
                def amm(c, qc=qc, kc=kc, vc=vc):
                    e = c.e
                    for mc in range(2):
                        e.matmul(ps_att[:, mc * CHUNK:(mc + 1) * CHUNK],
                                 kc[:, mc * 128:(mc + 1) * 128], qc,
                                 start=True, stop=True)
                    for mc in range(2):
                        e.transpose(ps_tp[:, mc * 128:(mc + 1) * 128],
                                    kc[:, mc * 128:(mc + 1) * 128], identt[:])
                        e.transpose(ps_tp[:, (2 + mc) * 128:(3 + mc) * 128],
                                    vc[:, mc * 128:(mc + 1) * 128], identt[:])
                s.tensor(amm)
                s = sp.stage(f"mask{h}_{ch}")
                def mask(c):
                    v = c.e
                    for mc in range(2):
                        v.tensor_mul(attb[:, mc * CHUNK:(mc + 1) * CHUNK],
                                     ps_att[:, mc * CHUNK:(mc + 1) * CHUNK],
                                     trilt[:, mc * CHUNK:(mc + 1) * CHUNK])
                    v.tensor_copy(tpb[:], ps_tp[:])
                s.vector(mask)
                s = sp.stage(f"num{h}_{ch}")
                def nmm(c, qc=qc):
                    e = c.e
                    for mc in range(2):
                        e.matmul(ps_num[:], tpb[:, (2 + mc) * 128:(3 + mc) * 128],
                                 attb[:, mc * CHUNK:(mc + 1) * CHUNK],
                                 start=(mc == 0), stop=False)
                    e.matmul(ps_num[:], Sb16[:], qc, start=False, stop=True)
                    for mc in range(2):
                        e.matmul(ps_den[:], onesb[:],
                                 attb[:, mc * CHUNK:(mc + 1) * CHUNK],
                                 start=(mc == 0), stop=False)
                    e.matmul(ps_den[:], Zb16[:], qc, start=False, stop=True)
                    for mc in range(2):
                        e.matmul(ps_S[:], tpb[:, mc * 128:(mc + 1) * 128],
                                 tpb[:, (2 + mc) * 128:(3 + mc) * 128],
                                 start=(mc == 0), stop=(mc == 1))
                s.tensor(nmm)
                s = sp.stage(f"upd{h}_{ch}")
                def upd(c, ch=ch, kc=kc, c0=c0):
                    v = c.e
                    v.tensor_copy(numsb[:, c0:c0 + CHUNK], ps_num[:])
                    v.tensor_copy(densb[0:1, c0:c0 + CHUNK], ps_den[:])
                    v.reduce_sum(zscr[:], kc, axis=AX)
                    if ch == 0:
                        v.tensor_copy(Ssb[:], ps_S[:])
                        v.drain()
                        v.tensor_copy(Zsb[:], zscr[:])
                    else:
                        v.tensor_add(Ssb[:], Ssb[:], ps_S[:])
                        v.drain()
                        v.tensor_add(Zsb[:], Zsb[:], zscr[:])
                    v.drain()
                    v.tensor_copy(Sb16[:], Ssb[:])
                    v.tensor_copy(Zb16[:], Zsb[:])
                s.vector(upd)
            s = sp.stage(f"dmax{h}")
            s.vector(lambda c: c.e.tensor_scalar_max(densb[:], densb[:], 1e-6))
            s = sp.stage(f"dln{h}")
            def dln(c):
                c.e.activation(densb[:], densb[:], AF.Ln)
                c.e.drain()
                c.e.activation(densb[:], densb[:], AF.Exp, scale=-1.0)
            s.scalar(dln)
            for tb in range(T // TT):
                s = sp.stage(f"rb{h}_{tb}")
                s.tensor(lambda c, tb=tb: c.e.matmul(
                    ps_rec[:], onesr[:], densb[0:1, tb * TT:(tb + 1) * TT],
                    start=True, stop=True))
                s = sp.stage(f"ro{h}_{tb}")
                s.vector(lambda c, tb=tb: c.e.tensor_mul(
                    outhb[:, tb * TT:(tb + 1) * TT],
                    numsb[:, tb * TT:(tb + 1) * TT], ps_rec[:]))
            s = sp.stage(f"hst{h}")
            s.sync(lambda c, h=h: c.dma(oo[h * 128:(h + 1) * 128, :],
                                        outhb[:]))
        sp.emit(ctx)
    return nc


def build_l1_back():
    nc = bass.Bass("TRN2", target_bir_lowering=False,
                   detect_race_conditions=DETECT_RACE)
    hoT = nc.dram_tensor("hoT", [D, TPC], BF16, kind="ExternalInput").ap()
    xT = nc.dram_tensor("xT", [D, TPC], F32, kind="ExternalInput").ap()
    woT = nc.dram_tensor("woutT", [D, D], F32, kind="ExternalInput").ap()
    w0T = nc.dram_tensor("w0T", [D, HID], F32, kind="ExternalInput").ap()
    b0d = nc.dram_tensor("b0", [HID, 1], F32, kind="ExternalInput").ap()
    w1T = nc.dram_tensor("w1T", [HID, D], F32, kind="ExternalInput").ap()
    b1d = nc.dram_tensor("b1", [D, 1], F32, kind="ExternalInput").ap()
    ones_col_d = nc.dram_tensor("ones_col", [128, 1], F32,
                                kind="ExternalInput").ap()
    ones_row_d = nc.dram_tensor("ones_row", [1, 128], F32,
                                kind="ExternalInput").ap()
    xo = nc.dram_tensor("x_out", [D, TPC], F32, kind="ExternalOutput").ap()

    QH = 1024
    NQ = HID // QH
    with ExitStack() as ctx:
        t = alloc_common(nc, ctx)
        sb = lambda nm, sh, dt=F32: ctx.enter_context(nc.sbuf_tensor(nm, sh, dt))
        psf = lambda nm, sh, dt=F32: ctx.enter_context(
            nc.psum_tensor(nm, sh, dt))
        xt = sb("xt", [128, KC * TT])
        hob = sb("hob", [128, KC * TT], BF16)
        ao = sb("ao", [128, KC * TT])
        xln = sb("xln", [128, KC * TT], BF16)
        scr = sb("scr", [128, 4 * TT])
        act_q = sb("act_q", [128, 8 * TT], BF16)
        actf4 = sb("actf4", [128, 4 * TT], F16)
        facf = sb("facf", [128, KC * TT], F16)
        b0t = sb("b0t", [128, 64])
        b1t = sb("b1t", [128, 16])
        wb = sb("wb", [128, KC * 512], BF16)
        wdb = sb("wdb", [128, 8 * 512], BF16)
        alloc_ln(nc, ctx, t, "ln", TT)
        ps_o = [psf(f"ps_o{i}", [128, TT]) for i in range(4)]
        sp = SP(nc)
        load_common(sp, t, ones_col_d, ones_row_d)
        s = sp.stage("bias_load")
        s.sync(lambda c: c.dma(b0t[:].rearrange("p (hc one) -> p hc one", one=1),
                               b0d.rearrange("(hc p) one -> p hc one", p=128), slow=True))
        s.sync(lambda c: c.dma(b1t[:].rearrange("p (kc one) -> p kc one", one=1),
                               b1d.rearrange("(kc p) one -> p kc one", p=128), slow=True))

        for tile in range(NTILES):
            off = tile * TT
            s = sp.stage(f"ld{tile}")
            s.sync(lambda c, off=off: c.dma(
                hob[:].rearrange("p (kc n) -> p kc n", kc=KC),
                hoT[:, off:off + TT].rearrange("(kc p) n -> p kc n", p=128)))
            s.sync(lambda c, off=off: c.dma(
                xt[:].rearrange("p (kc n) -> p kc n", kc=KC),
                xT[:, off:off + TT].rearrange("(kc p) n -> p kc n", p=128)))
            for ob in range(4):
                s = sp.stage(f"wo{tile}_{ob}")
                s.gpsimd(lambda c, ob=ob: c.dma(
                    wb[:].rearrange("p (kc o) -> p kc o", o=512),
                    woT[:, ob * 512:(ob + 1) * 512]
                    .rearrange("(kc p) o -> p kc o", p=128)))
                s = sp.stage(f"womm{tile}_{ob}")
                def mms(c):
                    for oc in range(4):
                        for k in range(KC):
                            c.e.matmul(
                                ps_o[oc][:],
                                wb[:, k * 512 + oc * 128:k * 512 + (oc + 1) * 128],
                                hob[:, k * TT:(k + 1) * TT],
                                start=(k == 0), stop=(k == KC - 1))
                s.tensor(mms)
                s = sp.stage(f"woev{tile}_{ob}")
                for oc in range(4):
                    s.vector(lambda c, ob=ob, oc=oc: c.e.tensor_copy(
                        ao[:, (ob * 4 + oc) * TT:(ob * 4 + oc + 1) * TT],
                        ps_o[oc][:]))
            aoc = [ao[:, i * TT:(i + 1) * TT] for i in range(KC)]
            emit_ln_stats(sp, nc, t, aoc, TT, "ln")
            emit_ln_apply(sp, f"lna1_{tile}", aoc, aoc,
                          t["ln_mb"], t["ln_rb"], scr, TT)
            emit_res_add(sp, f"res1_{tile}",
                         [xt[:, i * TT:(i + 1) * TT] for i in range(KC)], aoc)
            xc = [xt[:, i * TT:(i + 1) * TT] for i in range(KC)]
            emit_ln_stats(sp, nc, t, xc, TT, "ln")
            emit_ln_apply(sp, f"xln{tile}", xc,
                          [xln[:, i * TT:(i + 1) * TT] for i in range(KC)],
                          t["ln_mb"], t["ln_rb"], scr, TT)
            for q in range(NQ):
                for hb in range(2):
                    h0 = q * QH + hb * 512
                    s = sp.stage(f"fw{tile}_{q}_{hb}")
                    s.gpsimd(lambda c, h0=h0: c.dma(
                        wb[:].rearrange("p (kc o) -> p kc o", o=512),
                        w0T[:, h0:h0 + 512]
                        .rearrange("(kc p) o -> p kc o", p=128)))
                    s = sp.stage(f"f0{tile}_{q}_{hb}")
                    def gmm(c):
                        for oc in range(4):
                            for k in range(KC):
                                c.e.matmul(
                                    ps_o[oc][:],
                                    wb[:, k * 512 + oc * 128:
                                        k * 512 + (oc + 1) * 128],
                                    xln[:, k * TT:(k + 1) * TT],
                                    start=(k == 0), stop=(k == KC - 1))
                    s.tensor(gmm)
                    s = sp.stage(f"fgel{tile}_{q}_{hb}")
                    for oc in range(4):
                        def gel(c, q=q, hb=hb, oc=oc):
                            hc = q * 8 + hb * 4 + oc
                            c.e.activation(actf4[:, oc * TT:(oc + 1) * TT],
                                           ps_o[oc][:], AF.Gelu,
                                           bias=b0t[:, hc:hc + 1])
                        s.scalar(gel)
                    s = sp.stage(f"fcb{tile}_{q}_{hb}")
                    for oc in range(4):
                        s.vector(lambda c, hb=hb, oc=oc: c.e.tensor_copy(
                            act_q[:, (hb * 4 + oc) * TT:(hb * 4 + oc + 1) * TT],
                            actf4[:, oc * TT:(oc + 1) * TT]))
                for db in range(4):
                    s = sp.stage(f"fdw{tile}_{q}_{db}")
                    s.gpsimd(lambda c, q=q, db=db: c.dma(
                        wdb[:].rearrange("p (hc o) -> p hc o", o=512),
                        w1T[q * QH:(q + 1) * QH, db * 512:(db + 1) * 512]
                        .rearrange("(hc p) o -> p hc o", p=128)))
                    s = sp.stage(f"fdm{tile}_{q}_{db}")
                    def dmm(c):
                        for oc in range(4):
                            for k in range(8):
                                c.e.matmul(
                                    ps_o[oc][:],
                                    wdb[:, k * 512 + oc * 128:
                                         k * 512 + (oc + 1) * 128],
                                    act_q[:, k * TT:(k + 1) * TT],
                                    start=(k == 0), stop=(k == 7))
                    s.tensor(dmm)
                    s = sp.stage(f"fde{tile}_{q}_{db}")
                    for oc in range(4):
                        if q == 0:
                            s.vector(lambda c, db=db, oc=oc: c.e.tensor_copy(
                                ao[:, (db * 4 + oc) * TT:(db * 4 + oc + 1) * TT],
                                ps_o[oc][:]))
                        else:
                            s.vector(lambda c, db=db, oc=oc: c.e.tensor_add(
                                ao[:, (db * 4 + oc) * TT:(db * 4 + oc + 1) * TT],
                                ao[:, (db * 4 + oc) * TT:(db * 4 + oc + 1) * TT],
                                ps_o[oc][:]))
            s = sp.stage(f"fb1_{tile}")
            def b1a(c):
                v = c.e
                for r in range(4):
                    for j in range(4):
                        i = r * 4 + j
                        v.tensor_scalar_add(ao[:, i * TT:(i + 1) * TT],
                                            ao[:, i * TT:(i + 1) * TT],
                                            b1t[:, i:i + 1])
                    v.drain()
                    for j in range(4):
                        i = r * 4 + j
                        v.tensor_copy(facf[:, i * TT:(i + 1) * TT],
                                      ao[:, i * TT:(i + 1) * TT])
            s.vector(b1a)
            fc = [facf[:, i * TT:(i + 1) * TT] for i in range(KC)]
            emit_ln_stats(sp, nc, t, fc, TT, "ln")
            emit_ln_apply(sp, f"lna2_{tile}", fc, fc,
                          t["ln_mb"], t["ln_rb"], scr, TT)
            emit_res_add(sp, f"res2_{tile}",
                         [xt[:, i * TT:(i + 1) * TT] for i in range(KC)], fc)
            s = sp.stage(f"xst{tile}")
            s.sync(lambda c, off=off: c.dma(
                xo[:, off:off + TT].rearrange("(kc p) n -> p kc n", p=128),
                xt[:].rearrange("p (kc n) -> p kc n", kc=KC)))
        sp.emit(ctx)
    return nc


def facf_src(ao, db, oc):
    return ao[:, (db * 4 + oc) * TT:(db * 4 + oc + 1) * TT]


def _f32c(a):
    return np.ascontiguousarray(np.asarray(a, np.float32))


def kernel(L0_attn_x, L0_attn_time_decay, L0_attn_time_first,
           L0_attn_time_mix_k, L0_attn_time_mix_v, L0_attn_time_mix_r,
           L0_attn_Wk, L0_attn_Wv, L0_attn_Wr, L0_attn_Wo,
           L0_ffn_w_gate, L0_ffn_w_up, L0_ffn_w_down,
           L1_attn_W_qkv, L1_attn_W_out,
           L1_ffn_W0, L1_ffn_bias0, L1_ffn_W1, L1_ffn_bias1):
    B, T = 4, 4096
    x = _f32c(L0_attn_x)
    xTc = [np.ascontiguousarray(x[c // 2, (c % 2) * TPC:(c % 2 + 1) * TPC].T)
           for c in range(NCORES)]
    prevs = [np.zeros((D, 1), np.float32) if c % 2 == 0 else
             np.ascontiguousarray(x[c // 2, TPC - 1:TPC].T)
             for c in range(NCORES)]
    ones_col = np.ones((128, 1), np.float32)
    ones_row = np.ones((1, 128), np.float32)
    wkT = np.ascontiguousarray(_f32c(L0_attn_Wk).T)
    wvT = np.ascontiguousarray(_f32c(L0_attn_Wv).T)
    wrT = np.ascontiguousarray(_f32c(L0_attn_Wr).T)
    tm = np.ascontiguousarray(np.stack(
        [_f32c(L0_attn_time_mix_k).reshape(D),
         _f32c(L0_attn_time_mix_v).reshape(D),
         _f32c(L0_attn_time_mix_r).reshape(D)], axis=1))

    nc = build_l0_front()
    maps = [{"xT": xTc[c], "prev": prevs[c], "wkT": wkT, "wvT": wvT,
             "wrT": wrT, "tm": tm, "ones_col": ones_col,
             "ones_row": ones_row} for c in range(NCORES)]
    res = run_bass_kernel_spmd(nc, maps, core_ids=list(range(NCORES)))
    kf = np.stack([np.asarray(res.results[c]["k_out"])
                   for c in range(NCORES)])
    vf = np.stack([np.asarray(res.results[c]["v_out"])
                   for c in range(NCORES)])
    srf = np.stack([np.asarray(res.results[c]["sr_out"])
                    for c in range(NCORES)])

    def to_bdt(a):
        return np.concatenate([a[0::2], a[1::2]], axis=2)

    kB = to_bdt(kf)
    vB = to_bdt(vf)

    nc = build_l0_scan()
    td = _f32c(L0_attn_time_decay).reshape(D, 1)
    tf = _f32c(L0_attn_time_first).reshape(D, 1)
    maps = []
    for c in range(NCORES):
        b, dh = c // 2, c % 2
        sl = slice(dh * 1024, (dh + 1) * 1024)
        maps.append({"k_ch": np.ascontiguousarray(kB[b, sl]),
                     "v_ch": np.ascontiguousarray(vB[b, sl]),
                     "td": np.ascontiguousarray(td[sl]),
                     "tf": np.ascontiguousarray(tf[sl])})
    res = run_bass_kernel_spmd(nc, maps, core_ids=list(range(NCORES)))
    wkvB = np.zeros((B, D, T), np.float32)
    for c in range(NCORES):
        b, dh = c // 2, c % 2
        wkvB[b, dh * 1024:(dh + 1) * 1024] = np.asarray(
            res.results[c]["wkv_out"])

    nc = build_l0_back()
    woT = np.ascontiguousarray(_f32c(L0_attn_Wo).T)
    wgT = np.ascontiguousarray(_f32c(L0_ffn_w_gate).T)
    wuT = np.ascontiguousarray(_f32c(L0_ffn_w_up).T)
    wdT = np.ascontiguousarray(_f32c(L0_ffn_w_down).T)
    maps = []
    for c in range(NCORES):
        b, hf = c // 2, c % 2
        maps.append({"xT": xTc[c],
                     "wkvT": np.ascontiguousarray(
                         wkvB[b, :, hf * TPC:(hf + 1) * TPC]),
                     "srT": np.ascontiguousarray(srf[c]),
                     "woT": woT, "wgT": wgT, "wuT": wuT, "wdT": wdT,
                     "ones_col": ones_col, "ones_row": ones_row})
    res = run_bass_kernel_spmd(nc, maps, core_ids=list(range(NCORES)))
    x2c = [np.asarray(res.results[c]["x2_out"]) for c in range(NCORES)]

    nc = build_l1_qkv()
    wqkvT = np.ascontiguousarray(_f32c(L1_attn_W_qkv).T)
    inv = 1.0 / (10000.0 ** (np.arange(0, HDIM, 2, dtype=np.float32) / HDIM))
    pos = np.arange(T, dtype=np.float32)
    fr = np.outer(pos, inv)
    emb = np.concatenate([fr, fr], axis=-1)
    cosT = np.ascontiguousarray(np.cos(emb).T.astype(np.float32))
    sinT = np.ascontiguousarray(np.sin(emb).T.astype(np.float32))
    P = np.zeros((128, 128), np.float32)
    for i in range(64):
        P[2 * i, 2 * i + 1] = -1.0
        P[2 * i + 1, 2 * i] = 1.0
    rotP = np.ascontiguousarray(P.T)
    maps = []
    for c in range(NCORES):
        hf = c % 2
        maps.append({"xT": x2c[c], "wqkvT": wqkvT,
                     "cosd": np.ascontiguousarray(
                         cosT[:, hf * TPC:(hf + 1) * TPC]),
                     "sind": np.ascontiguousarray(
                         sinT[:, hf * TPC:(hf + 1) * TPC]),
                     "rotP": rotP, "ones_col": ones_col,
                     "ones_row": ones_row})
    res = run_bass_kernel_spmd(nc, maps, core_ids=list(range(NCORES)))
    qf = np.stack([np.asarray(res.results[c]["q_out"])
                   for c in range(NCORES)])
    kf1 = np.stack([np.asarray(res.results[c]["k_out"])
                    for c in range(NCORES)])
    vf1 = np.stack([np.asarray(res.results[c]["v_out"])
                    for c in range(NCORES)])
    qB = to_bdt(qf)
    kB1 = to_bdt(kf1)
    vB1 = to_bdt(vf1)

    nc = build_l1_attn()
    trilT = np.ascontiguousarray(np.triu(np.ones((CHUNK, CHUNK), np.float32)))
    ident = np.eye(128, dtype=np.float32)
    maps = []
    for c in range(NCORES):
        b, hg = c // 2, c % 2
        sl = slice(hg * 1024, (hg + 1) * 1024)
        maps.append({"q_in": np.ascontiguousarray(qB[b, sl]),
                     "k_in": np.ascontiguousarray(kB1[b, sl]),
                     "v_in": np.ascontiguousarray(vB1[b, sl]),
                     "trilT": trilT, "ident": ident, "ones_row": ones_row})
    res = run_bass_kernel_spmd(nc, maps, core_ids=list(range(NCORES)))
    hoB = np.zeros((B, D, T), np.dtype(BF))
    for c in range(NCORES):
        b, hg = c // 2, c % 2
        hoB[b, hg * 1024:(hg + 1) * 1024] = np.asarray(
            res.results[c]["ho_out"])

    nc = build_l1_back()
    woutT = np.ascontiguousarray(_f32c(L1_attn_W_out).T)
    w0T = np.ascontiguousarray(_f32c(L1_ffn_W0).T)
    w1T = np.ascontiguousarray(_f32c(L1_ffn_W1).T)
    b0 = _f32c(L1_ffn_bias0).reshape(HID, 1)
    b1 = _f32c(L1_ffn_bias1).reshape(D, 1)
    maps = []
    for c in range(NCORES):
        b, hf = c // 2, c % 2
        maps.append({"hoT": np.ascontiguousarray(
                         hoB[b, :, hf * TPC:(hf + 1) * TPC]),
                     "xT": x2c[c], "woutT": woutT, "w0T": w0T, "b0": b0,
                     "w1T": w1T, "b1": b1, "ones_col": ones_col,
                     "ones_row": ones_row})
    res = run_bass_kernel_spmd(nc, maps, core_ids=list(range(NCORES)))
    out = np.zeros((B, T, D), np.float32)
    for c in range(NCORES):
        b, hf = c // 2, c % 2
        out[b, hf * TPC:(hf + 1) * TPC] = np.asarray(
            res.results[c]["x_out"]).T
    return out
